# version 01
# baseline (speedup 1.0000x reference)
import numpy as np
from contextlib import ExitStack
import concourse.bass as bass
import concourse.mybir as mybir

F32 = mybir.dt.float32
I32 = mybir.dt.int32
U32 = mybir.dt.uint32
ALU = mybir.AluOpType
AF = mybir.ActivationFunctionType
AX = mybir.AxisListType

D = 1024
L = 4096
NCORE = 8
PT = D // 128

ENGS = ("sync", "vector", "scalar", "tensor", "gpsimd")


class Prog:

    ENG_SEM = {"vector": "s_dve", "scalar": "s_act", "tensor": "s_pe"}

    def __init__(self, nc, paranoid=False):
        self.nc = nc
        self.paranoid = paranoid
        self.q = {e: [] for e in ENGS}
        self.es = ExitStack()
        self.sems = {}
        self.cnt = {}
        self.consts = {}
        self.bufs = {}
        for s in ("s_pe", "s_act", "s_dve", "s_gp", "s_cc"):
            self.sem(s)
        self.DMA_R = 6
        self.dma_ring = {}
        self.dma_idx = {}
        self.dma_recs = {}
        for q in ("sync", "gpsimd", "vector", "scalar"):
            self.dma_ring[q] = []
            self.dma_idx[q] = 0
            self.dma_recs[q] = []
            for r in range(self.DMA_R):
                self.dma_ring[q].append(self.sem("dq_%s_%d" % (q, r)) and
                                        ("dq_%s_%d" % (q, r)))

    def sem(self, name):
        h = self.es.enter_context(self.nc.semaphore(name))
        self.sems[name] = h
        self.cnt[name] = 0
        return h

    def sb(self, name, shape, dtype=F32, stack=None):
        return (stack or self.es).enter_context(
            self.nc.sbuf_tensor(name, list(shape), dtype))

    def ps(self, name, shape=(128, 512), dtype=F32):
        return self.es.enter_context(self.nc.psum_tensor(name, list(shape), dtype))

    def on(self, eng, fn, waits=(), inc=None, inc_n=1):
        waits = [w for w in waits]
        nw = []
        for s, v in waits:
            if s.startswith("d_"):
                q = s[2:]
                n_dma = v // 16
                recs = self.dma_recs[q][:n_dma][-self.DMA_R:]
                best = {}
                for rs, rv in recs:
                    best[rs] = max(best.get(rs, 0), rv)
                nw += list(best.items())
            else:
                nw.append((s, v))
        waits = nw
        if inc in ("d_sync", "d_gpsimd", "d_vector", "d_scalar"):
            q = inc[2:]
            assert inc_n == 16
            idx = self.dma_idx[q]
            self.dma_idx[q] = idx + 1
            sem = self.dma_ring[q][idx % self.DMA_R]
            use = idx // self.DMA_R + 1
            if use > 1:
                waits = waits + [(sem, 16 * (use - 1))]
            self.dma_recs[q].append((sem, 16 * use))
            inc = sem
        if self.paranoid and eng in self.ENG_SEM:
            es = self.ENG_SEM[eng]
            waits = list(waits) + [(es, self.cnt[es])]
            if inc is None:
                inc = es
        if inc is not None:
            self.cnt[inc] += inc_n
        self.wait_log = getattr(self, "wait_log", [])
        for s, v in waits:
            self.wait_log.append((eng, s, v))
        self.op_log = getattr(self, "op_log", {e: [] for e in ENGS})
        self.op_log[eng].append((tuple(waits), inc, inc_n))
        sems = self.sems
        def f(e, fn=fn, waits=tuple(waits), inc=inc, inc_n=inc_n):
            for s, v in waits:
                if v > 0:
                    e.wait_ge(sems[s], v)
            inst = fn(e)
            if inc is not None:
                assert inst is not None, "inc requires an instruction"
                inst.then_inc(sems[inc], inc_n)
        self.q[eng].append(f)

    def now(self, *names):
        out = []
        for n in names:
            if n.startswith("d_"):
                out.append((n, 16 * self.dma_idx[n[2:]]))
            else:
                out.append((n, self.cnt[n]))
        return out

    def dma(self, eng, out, in_, waits=()):
        self.on(eng, lambda e, out=out, in_=in_: e.dma_start(out=out, in_=in_),
                waits=waits, inc="d_" + eng, inc_n=16)
        return [self.dma_recs[eng][-1]]

    def check_waits(self):
        bad = [(e, s, v) for (e, s, v) in getattr(self, "wait_log", [])
               if v > self.cnt[s]]
        if bad:
            print("IMPOSSIBLE WAITS:", bad[:10])
        else:
            print("wait check ok (%d waits)" % len(getattr(self, "wait_log", [])))

    def deadlock_check(self):
        import collections
        q = {e: collections.deque(self.op_log.get(e, [])) for e in ENGS}
        cnt = {s: 0 for s in self.cnt}
        progress = True
        while progress:
            progress = False
            for e in ENGS:
                while q[e]:
                    waits, inc, inc_n = q[e][0]
                    if all(cnt[s] >= v for s, v in waits):
                        if inc:
                            cnt[inc] += inc_n
                        q[e].popleft()
                        progress = True
                    else:
                        break
        stuck = {e: len(q[e]) for e in ENGS if q[e]}
        if stuck:
            print("DEADLOCK: remaining ops per engine:", stuck)
            for e in ENGS:
                if q[e]:
                    waits, inc, inc_n = q[e][0]
                    print("  %s head waits:" % e,
                          [(s, v, cnt[s]) for s, v in waits if cnt[s] < v])
        else:
            print("deadlock check ok")

    def barrier(self):
        w = self.now("s_pe", "s_act", "s_dve") + self.now("d_sync", "d_gpsimd")
        for e in ENGS:
            self.on(e, lambda e_: None, waits=w)

    def finalize(self):
        nc = self.nc
        q = self.q
        with nc.Block() as block:
            @block.sync
            def _(e):
                for f in q["sync"]: f(e)
            @block.vector
            def _(e):
                for f in q["vector"]: f(e)
            @block.scalar
            def _(e):
                for f in q["scalar"]: f(e)
            @block.tensor
            def _(e):
                for f in q["tensor"]: f(e)
            @block.gpsimd
            def _(e):
                for f in q["gpsimd"]: f(e)


class Bank:
    __slots__ = ("t", "free")
    def __init__(self, t):
        self.t = t
        self.free = []

    def use(self, waits):
        self.free = list(waits)


def fm_ap(dram, t0, W, pt0=0, npt=PT):
    Ld = dram.shape[1]
    return bass.AP(dram, (pt0 * 128) * Ld + t0,
                   [[Ld, 128], [128 * Ld, npt], [1, W]])


def ln_chunk(p, banks, xsrc, t0, W, xch, xln, rows, affine=None, sq=None,
             eps=1e-5, ld_eng="sync", xch_ready=None, outW=None, outoff=0):
    prev_dve = p.bufs.get("ln_prev_dve", [])
    prev_pe = p.bufs.get("ln_prev_pe", [])
    if xch_ready is None:
        p.on(ld_eng, lambda e: e.dma_start(out=xch[:, 0:PT * W], in_=fm_ap(xsrc, t0, W)),
             waits=prev_dve, inc="d_" + ld_eng, inc_n=16)
        ld = p.now("d_" + ld_eng)
    else:
        ld = list(xch_ready)
    p.on("scalar", lambda e: e.activation(out=sq[:, 0:PT * W], in_=xch[:, 0:PT * W],
                                          func=AF.Square), waits=list(ld) + list(prev_pe),
         inc="s_act")
    sq_done = p.now("s_act")
    if outW is None:
        outW = W
    def osl(pt):
        return slice(pt * outW + outoff, pt * outW + outoff + W)
    b5, b6 = banks[5], banks[6]
    for bk, srcT, w in ((b5, xch, ld), (b6, sq, sq_done)):
        for pt in range(PT):
            p.on("tensor",
                 lambda e, bk=bk, srcT=srcT, pt=pt: e.matmul(
                     bk.t[0:1, 0:W], p.consts["ones_col"][:, 0:1],
                     srcT[:, pt * W:(pt + 1) * W], start=(pt == 0), stop=(pt == PT - 1)),
                 waits=(list(w) + bk.free) if pt == 0 else (),
                 inc="s_pe" if pt == PT - 1 else None)
    mm_done = p.now("s_pe")
    p.on("scalar", lambda e: e.activation(out=rows[0:1, 0:W], in_=b5.t[0:1, 0:W],
                                          func=AF.Copy, scale=1.0 / D),
         waits=mm_done, inc="s_act")
    p.on("scalar", lambda e: e.activation(out=rows[0:1, W:2 * W], in_=b6.t[0:1, 0:W],
                                          func=AF.Copy, scale=1.0 / D), inc="s_act")
    b5.use(p.now("s_act")); b6.use(p.now("s_act"))
    p.on("scalar", lambda e: e.activation(out=rows[0:1, 2 * W:3 * W], in_=rows[0:1, 0:W],
                                          func=AF.Square), inc="s_act")
    p.on("vector", lambda e: e.tensor_tensor(rows[0:1, W:2 * W], rows[0:1, W:2 * W],
                                             rows[0:1, 2 * W:3 * W], ALU.subtract),
         waits=p.now("s_act"), inc="s_dve")
    p.on("scalar", lambda e: e.activation(out=rows[0:1, W:2 * W], in_=rows[0:1, W:2 * W],
                                          func=AF.Ln, bias=p.consts["cscal"][0:1, 0:1]),
         waits=p.now("s_dve"), inc="s_act")
    p.on("scalar", lambda e: e.activation(out=rows[0:1, W:2 * W], in_=rows[0:1, W:2 * W],
                                          func=AF.Exp, scale=-0.5), inc="s_act")
    rowdone = p.now("s_act")
    p.on("tensor", lambda e: e.matmul(b5.t[:, 0:W], p.consts["ones_row"][0:1, :],
                                      rows[0:1, 0:W], start=True, stop=True),
         waits=rowdone + b5.free, inc="s_pe")
    p.on("tensor", lambda e: e.matmul(b6.t[:, 0:W], p.consts["ones_row"][0:1, :],
                                      rows[0:1, W:2 * W], start=True, stop=True),
         waits=b6.free, inc="s_pe")
    bc_done = p.now("s_pe")
    mean_b, rstd_b = p.bufs["mean_b"], p.bufs["rstd_b"]
    p.on("scalar", lambda e: e.activation(out=mean_b[:, 0:W], in_=b5.t[:, 0:W],
                                          func=AF.Copy), waits=bc_done, inc="s_act")
    p.on("scalar", lambda e: e.activation(out=rstd_b[:, 0:W], in_=b6.t[:, 0:W],
                                          func=AF.Copy), inc="s_act")
    b5.use(p.now("s_act")); b6.use(p.now("s_act"))
    bdone = p.now("s_act")
    for pt in range(PT):
        p.on("vector", lambda e, pt=pt: e.tensor_tensor(
            xln[:, osl(pt)], xch[:, pt * W:(pt + 1) * W], mean_b[:, 0:W],
            ALU.subtract), waits=bdone if pt == 0 else ())
        p.on("vector", lambda e, pt=pt: e.tensor_tensor(
            xln[:, osl(pt)], xln[:, osl(pt)], rstd_b[:, 0:W],
            ALU.mult), inc="s_dve" if pt == PT - 1 else None)
    done = p.now("s_dve")
    if affine is not None:
        wcol, bcol = affine
        for pt in range(PT):
            p.on("scalar", lambda e, pt=pt: e.activation(
                out=xln[:, osl(pt)], in_=xln[:, osl(pt)],
                func=AF.Identity, scale=wcol[:, pt:pt + 1], bias=bcol[:, pt:pt + 1]),
                waits=done if pt == 0 else (),
                inc="s_act" if pt == PT - 1 else None)
        done = p.now("s_act")
    p.bufs["ln_prev_dve"] = p.now("s_dve")
    p.bufs["ln_prev_pe"] = mm_done
    return done


def phase_attn(p, banks, dr, xsrc_name="x_in", xdst_name="xres"):
    nc = p.nc
    W = 512
    NCH = L // W
    st = ExitStack()
    Qs = p.sb("at_Q", [64, L], stack=st)
    Ks = p.sb("at_K", [64, L], stack=st)
    VT = p.sb("at_VT", [128, 32 * 64], stack=st)
    Z = p.sb("at_Z", [64, L], stack=st)
    xch = p.sb("at_xch", [128, PT * W], stack=st)
    xln = p.sb("at_xln", [128, PT * W], stack=st)
    sq = p.sb("at_sq", [128, PT * W], stack=st)
    rows = p.sb("at_rows", [1, 3 * W], stack=st)
    p.bufs["mean_b"] = p.sb("at_meanb", [128, W], stack=st)
    p.bufs["rstd_b"] = p.sb("at_rstdb", [128, W], stack=st)
    wq = p.sb("at_wq", [128, PT * 64], stack=st)
    wk = p.sb("at_wk", [128, PT * 64], stack=st)
    wv = p.sb("at_wv", [128, PT * 64], stack=st)
    wo = p.sb("at_wo", [64, D], stack=st)
    bcols = p.sb("at_bcols", [64, 4], stack=st)
    bo_t = p.sb("at_bot", [128, PT], stack=st)
    bv_b = p.sb("at_bvb", [128, 64], stack=st)
    mask = p.sb("at_mask", [128, 128], stack=st)
    Srow = p.sb("at_Srow", [128, L], stack=st)
    PTb = p.sb("at_PTb", [128, 2 * 128], stack=st)
    mcol = p.sb("at_mcol", [128, 4], stack=st)

    for wsb, name in ((wq, "wq"), (wk, "wk"), (wv, "wv")):
        p.dma("sync", wsb[:, 0:PT * 64],
              bass.AP(dr[name], 0, [[64, 128], [128 * 64, PT], [1, 64]]))
    p.dma("sync", wo[:, :], dr["wo"].ap())
    p.dma("sync", bcols[:, 0:1], bass.AP(dr["bq"], 0, [[1, 64], [1, 1]]))
    p.dma("sync", bcols[:, 1:2], bass.AP(dr["bk"], 0, [[1, 64], [1, 1]]))
    p.dma("sync", bv_b[:, :], bass.AP(dr["bv"], 0, [[0, 128], [1, 64]]))
    p.dma("sync", bo_t[:, :], dr["bo_t"].ap())
    p.dma("sync", mask[:, :], dr["maskdiag"].ap())
    wld = p.now("d_sync")
    p.on("vector", lambda e: e.tensor_scalar(bcols[:, 2:3], bcols[:, 0:1], 0.125, None,
                                             ALU.mult), waits=wld, inc="s_dve")
    bqs_done = p.now("s_dve")

    b0, b1, b2 = banks[0], banks[1], banks[2]
    for ch in range(NCH):
        t0 = ch * W
        xdone = ln_chunk(p, banks, dr[xsrc_name], t0, W, xch, xln, rows, sq=sq)
        for kt in range(PT):
            p.on("tensor", lambda e, kt=kt: e.matmul(
                b0.t[0:64, 0:W], wq[:, kt * 64:(kt + 1) * 64],
                xln[:, kt * W:(kt + 1) * W], start=(kt == 0), stop=(kt == PT - 1)),
                waits=(list(xdone) + list(wld) + list(bqs_done) + b0.free) if kt == 0 else (),
                inc="s_pe" if kt == PT - 1 else None)
        p.on("scalar", lambda e, ch=ch: e.activation(
            out=Qs[:, ch * W:(ch + 1) * W], in_=b0.t[0:64, 0:W], func=AF.Identity,
            scale=0.125, bias=bcols[:, 2:3]), waits=p.now("s_pe"), inc="s_act")
        b0.use(p.now("s_act"))
        for kt in range(PT):
            p.on("tensor", lambda e, kt=kt: e.matmul(
                b1.t[0:64, 0:W], wk[:, kt * 64:(kt + 1) * 64],
                xln[:, kt * W:(kt + 1) * W], start=(kt == 0), stop=(kt == PT - 1)),
                waits=b1.free if kt == 0 else (), inc="s_pe" if kt == PT - 1 else None)
        p.on("scalar", lambda e, ch=ch: e.activation(
            out=Ks[:, ch * W:(ch + 1) * W], in_=b1.t[0:64, 0:W], func=AF.Identity,
            bias=bcols[:, 1:2]), waits=p.now("s_pe"), inc="s_act")
        b1.use(p.now("s_act"))
        for tl in range(W // 128):
            for kt in range(PT):
                p.on("tensor", lambda e, tl=tl, kt=kt: e.matmul(
                    b2.t[:, 0:64], xln[:, kt * W + tl * 128: kt * W + (tl + 1) * 128],
                    wv[:, kt * 64:(kt + 1) * 64], start=(kt == 0), stop=(kt == PT - 1)),
                    waits=b2.free if kt == 0 else (), inc="s_pe" if kt == PT - 1 else None)
            gt = ch * (W // 128) + tl
            p.on("vector", lambda e, gt=gt: e.tensor_tensor(
                VT[:, gt * 64:(gt + 1) * 64], b2.t[:, 0:64], bv_b[:, :], ALU.add),
                waits=p.now("s_pe"), inc="s_dve")
            b2.use(p.now("s_dve"))
    qkv_done = p.now("s_dve", "s_act")
    if "dQ" in dr:
        p.on("gpsimd", lambda e: e.dma_start(out=dr["dQ"].ap(), in_=Qs[:, :]),
             waits=qkv_done, inc="d_gpsimd", inc_n=16)
        p.on("gpsimd", lambda e: e.dma_start(out=dr["dK"].ap(), in_=Ks[:, :]),
             inc="d_gpsimd", inc_n=16)
        p.on("gpsimd", lambda e: e.dma_start(out=dr["dVT"].ap(), in_=VT[:, :]),
             inc="d_gpsimd", inc_n=16)

    b4, b7 = banks[4], banks[7]
    for qi in range(32):
        klen = (qi + 1) * 128
        off = 0
        prev_dve = p.bufs.get("attn_prev_dve", [])
        while off < klen:
            cw = min(512, klen - off)
            bk = banks[off // 512 % 2]
            p.on("tensor", lambda e, qi=qi, off=off, cw=cw, bk=bk: e.matmul(
                bk.t[:, 0:cw], Qs[:, qi * 128:(qi + 1) * 128], Ks[:, off:off + cw],
                start=True, stop=True),
                waits=(list(qkv_done) + bk.free), inc="s_pe")
            p.on("scalar", lambda e, off=off, cw=cw, bk=bk: e.activation(
                out=Srow[:, off:off + cw], in_=bk.t[:, 0:cw], func=AF.Copy),
                waits=list(p.now("s_pe")) + (list(prev_dve) if off == 0 else []),
                inc="s_act")
            bk.use(p.now("s_act"))
            off += cw
        p.on("vector", lambda e, qi=qi: e.tensor_tensor(
            Srow[:, qi * 128:(qi + 1) * 128], Srow[:, qi * 128:(qi + 1) * 128],
            mask[:, :], ALU.add), waits=p.now("s_act"), inc="s_dve")
        p.on("vector", lambda e, klen=klen: e.tensor_reduce(
            mcol[:, 0:1], Srow[:, 0:klen], AX.X, ALU.max), inc="s_dve")
        p.on("vector", lambda e: e.tensor_scalar(mcol[:, 1:2], mcol[:, 0:1], -1.0, None,
                                                 ALU.mult), inc="s_dve")
        p.on("scalar", lambda e, klen=klen: e.activation(
            out=Srow[:, 0:klen], in_=Srow[:, 0:klen], func=AF.Exp, bias=mcol[:, 1:2]),
            waits=p.now("s_dve"), inc="s_act")
        p.on("vector", lambda e, klen=klen: e.tensor_reduce(
            mcol[:, 2:3], Srow[:, 0:klen], AX.X, ALU.add), waits=p.now("s_act"),
            inc="s_dve")
        p.on("scalar", lambda e: e.activation(out=mcol[:, 3:4], in_=mcol[:, 2:3],
                                              func=AF.Ln), waits=p.now("s_dve"),
             inc="s_act")
        p.on("scalar", lambda e: e.activation(out=mcol[:, 3:4], in_=mcol[:, 3:4],
                                              func=AF.Exp, scale=-1.0), inc="s_act")
        p.on("vector", lambda e, klen=klen: e.tensor_scalar(
            Srow[:, 0:klen], Srow[:, 0:klen], mcol[:, 3:4], None, ALU.mult),
            waits=p.now("s_act"), inc="s_dve")
        pn = p.now("s_dve")
        p.bufs["attn_prev_dve"] = pn
        if "dSrow" in dr and qi == dr.get("dbg_qi", 1):
            p.on("gpsimd", lambda e, klen=klen: e.dma_start(
                out=bass.AP(dr["dSrow"], 0, [[L, 128], [1, klen]]),
                in_=Srow[:, 0:klen]), waits=pn, inc="d_gpsimd", inc_n=16)
        for j in range(qi + 1):
            p.on("tensor", lambda e, j=j: e.transpose(
                b4.t[:, 0:128], Srow[:, j * 128:(j + 1) * 128], p.consts["ident"][:, :]),
                waits=(list(pn) + b4.free) if j == 0 else b4.free, inc="s_pe")
            buf = j % 2
            p.on("scalar", lambda e, buf=buf: e.activation(
                out=PTb[:, buf * 128:(buf + 1) * 128], in_=b4.t[:, 0:128], func=AF.Copy),
                waits=p.now("s_pe"), inc="s_act")
            b4.use(p.now("s_act"))
            p.on("tensor", lambda e, j=j, buf=buf: e.matmul(
                b7.t[0:64, 0:128], VT[:, j * 64:(j + 1) * 64],
                PTb[:, buf * 128:(buf + 1) * 128], start=(j == 0), stop=(j == qi)),
                waits=(list(p.now("s_act")) + (b7.free if j == 0 else [])),
                inc="s_pe" if j == qi else None)
        p.on("scalar", lambda e, qi=qi: e.activation(
            out=Z[:, qi * 128:(qi + 1) * 128], in_=b7.t[0:64, 0:128], func=AF.Copy),
            waits=p.now("s_pe"), inc="s_act")
        b7.use(p.now("s_act"))
    z_done = p.now("s_act")
    if "dZ" in dr:
        p.on("gpsimd", lambda e: e.dma_start(out=dr["dZ"].ap(), in_=Z[:, :]),
             waits=z_done, inc="d_gpsimd", inc_n=16)

    for ch in range(NCH):
        t0 = ch * W
        prev_guard = p.now("d_gpsimd")
        p.on("sync", lambda e, t0=t0: e.dma_start(out=xch[:, 0:PT * W],
                                                  in_=fm_ap(dr[xsrc_name], t0, W)),
             waits=p.now("s_dve"), inc="d_sync", inc_n=16)
        xl = p.now("d_sync")
        for et in range(PT):
            bk = banks[et % 2]
            p.on("tensor", lambda e, et=et, ch=ch, bk=bk: e.matmul(
                bk.t[:, 0:W], wo[:, et * 128:(et + 1) * 128], Z[:, ch * W:(ch + 1) * W],
                start=True, stop=True),
                waits=(list(z_done) + bk.free), inc="s_pe")
            p.on("scalar", lambda e, et=et, bk=bk: e.activation(
                out=sq[:, et * W:(et + 1) * W], in_=bk.t[:, 0:W], func=AF.Identity,
                bias=bo_t[:, et:et + 1]),
                waits=list(p.now("s_pe")) + (list(p.bufs.get("op_prev_dve", [])) if et == 0 else []),
                inc="s_act")
            bk.use(p.now("s_act"))
            p.on("vector", lambda e, et=et: e.tensor_tensor(
                xln[:, et * W:(et + 1) * W], sq[:, et * W:(et + 1) * W],
                xch[:, et * W:(et + 1) * W], ALU.add),
                waits=(list(p.now("s_act")) + list(xl) + (list(prev_guard) if et == 0 else [])),
                inc="s_dve")
        p.bufs["op_prev_dve"] = p.now("s_dve")
        p.on("gpsimd", lambda e, t0=t0: e.dma_start(out=fm_ap(dr[xdst_name], t0, W),
                                                    in_=xln[:, 0:PT * W]),
             waits=p.now("s_dve"), inc="d_gpsimd", inc_n=16)
        if "dOP" in dr and ch == 0:
            p.on("gpsimd", lambda e: e.dma_start(out=dr["dOP"].ap(), in_=sq[:, 0:PT * W]),
                 waits=p.now("s_dve"), inc="d_gpsimd", inc_n=16)
    st.close()
    return p.now("d_gpsimd")


def phase_rwkv(p, banks, dr, xsrc_name="xres", xdst_name="xout"):
    W = 512
    NCH = L // W
    st = ExitStack()
    xch = p.sb("rw_xch", [128, PT * W], stack=st)
    xln = p.sb("rw_xln", [128, PT * W], stack=st)
    sq = p.sb("rw_sq", [128, PT * W], stack=st)
    rows = p.sb("rw_rows", [1, 3 * W], stack=st)
    p.bufs["mean_b"] = p.sb("rw_meanb", [128, W], stack=st)
    p.bufs["rstd_b"] = p.sb("rw_rstdb", [128, W], stack=st)
    xab = p.sb("rw_xab", [128, PT * (W + 1)], stack=st)
    xr = p.sb("rw_xr", [128, PT * W], stack=st)
    xk = p.sb("rw_xk", [128, PT * W], stack=st)
    rr = xln
    wr = p.sb("rw_wr", [128, PT * 1024], stack=st)
    wk = p.sb("rw_wk", [128, PT * 1024], stack=st)
    wv = p.sb("rw_wv", [128, PT * 1024], stack=st)
    mu = p.sb("rw_mu", [128, 4 * PT], stack=st)
    lnb = dr["lnbuf"]

    def wload(dst, name):
        p.dma("sync", dst[:, 0:PT * 1024],
              bass.AP(dr[name], 0, [[1024, 128], [128 * 1024, PT], [1, 1024]]))
    wload(wr, "wrT"); wload(wk, "wkT"); wload(wv, "wvT")
    p.dma("sync", mu[:, 0:PT], dr["mur_t"].ap())
    p.dma("sync", mu[:, PT:2 * PT], dr["muk_t"].ap())
    wld = p.now("d_sync")
    p.on("vector", lambda e: e.tensor_scalar(mu[:, 2 * PT:3 * PT], mu[:, 0:PT],
                                             -1.0, 1.0, ALU.mult, ALU.add),
         waits=wld, inc="s_dve")
    p.on("vector", lambda e: e.tensor_scalar(mu[:, 3 * PT:4 * PT], mu[:, PT:2 * PT],
                                             -1.0, 1.0, ALU.mult, ALU.add), inc="s_dve")
    mu_done = p.now("s_dve")

    for ch in range(NCH):
        t0 = ch * W
        xdone = ln_chunk(p, banks, dr[xsrc_name], t0, W, xch, xln, rows, sq=sq)
        p.on("gpsimd", lambda e, t0=t0: e.dma_start(out=fm_ap(lnb, t0, W),
                                                    in_=xln[:, 0:PT * W]),
             waits=xdone, inc="d_gpsimd", inc_n=16)
    ln_done = p.now("d_gpsimd")

    for ch in range(NCH):
        t0 = ch * W
        guard = p.bufs.get("rw_prev", [])
        if ch == 0:
            p.on("vector", lambda e: e.memset(
                bass.AP(xab.ap().tensor, 0, [[PT * (W + 1), 128], [W + 1, PT], [1, 1]]), 0.0),
                waits=list(ln_done) + guard, inc="s_dve")
            p.on("sync", lambda e: e.dma_start(
                out=bass.AP(xab.ap().tensor, 1, [[PT * (W + 1), 128], [W + 1, PT], [1, W]]),
                in_=fm_ap(lnb, 0, W)), waits=list(ln_done) + guard,
                inc="d_sync", inc_n=16)
        else:
            p.on("sync", lambda e, t0=t0: e.dma_start(
                out=bass.AP(xab.ap().tensor, 0, [[PT * (W + 1), 128], [W + 1, PT], [1, W + 1]]),
                in_=fm_ap(lnb, t0 - 1, W + 1)), waits=list(ln_done) + guard,
                inc="d_sync", inc_n=16)
        p.on("sync", lambda e, t0=t0: e.dma_start(out=xch[:, 0:PT * W],
                                                  in_=fm_ap(dr[xsrc_name], t0, W)),
             waits=(list(ln_done) if ch == 0 else list(guard)), inc="d_sync", inc_n=16)
        ld = p.now("d_sync", "s_dve")
        for pt in range(PT):
            a = xab[:, pt * (W + 1) + 1:(pt + 1) * (W + 1)]
            b = xab[:, pt * (W + 1):(pt + 1) * (W + 1) - 1]
            p.on("vector", lambda e, pt=pt, a=a: e.tensor_scalar(
                xr[:, pt * W:(pt + 1) * W], a, mu[:, pt:pt + 1], None, ALU.mult),
                waits=(list(ld) + list(mu_done)) if pt == 0 else ())
            p.on("vector", lambda e, pt=pt, a=a, b=b: e.scalar_tensor_tensor(
                xr[:, pt * W:(pt + 1) * W], b, mu[:, 2 * PT + pt:2 * PT + pt + 1],
                xr[:, pt * W:(pt + 1) * W], ALU.mult, ALU.add))
            p.on("vector", lambda e, pt=pt, a=a: e.tensor_scalar(
                xk[:, pt * W:(pt + 1) * W], a, mu[:, PT + pt:PT + pt + 1], None, ALU.mult))
            p.on("vector", lambda e, pt=pt, a=a, b=b: e.scalar_tensor_tensor(
                xk[:, pt * W:(pt + 1) * W], b, mu[:, 3 * PT + pt:3 * PT + pt + 1],
                xk[:, pt * W:(pt + 1) * W], ALU.mult, ALU.add),
                inc="s_dve" if pt == PT - 1 else None)
        mix_done = p.now("s_dve")
        def proj(dst, wsb, src, func, ready, tag):
            for et in range(PT):
                bk = banks[et % 2]
                for kt in range(PT):
                    p.on("tensor", lambda e, et=et, kt=kt, bk=bk, wsb=wsb, src=src: e.matmul(
                        bk.t[:, 0:W], wsb[:, kt * 1024 + et * 128:kt * 1024 + (et + 1) * 128],
                        src[:, kt * W:(kt + 1) * W], start=(kt == 0), stop=(kt == PT - 1)),
                        waits=(list(ready) + bk.free) if kt == 0 else (),
                        inc="s_pe" if kt == PT - 1 else None)
                p.on("scalar", lambda e, et=et, bk=bk, dst=dst, func=func: e.activation(
                    out=dst[:, et * W:(et + 1) * W], in_=bk.t[:, 0:W], func=func),
                    waits=p.now("s_pe"), inc="s_act")
                bk.use(p.now("s_act"))
            return p.now("s_act")
        rdone = proj(rr, wr, xr, AF.Sigmoid, mix_done, "r")
        kdone = proj(xr, wk, xk, AF.Relu, mix_done, "k")
        p.on("scalar", lambda e: e.activation(out=xr[:, 0:PT * W], in_=xr[:, 0:PT * W],
                                              func=AF.Square), waits=kdone, inc="s_act")
        ka_done = p.now("s_act")
        vdone = proj(xk, wv, xr, AF.Copy, ka_done, "v")
        for pt in range(PT):
            p.on("vector", lambda e, pt=pt: e.tensor_tensor(
                xk[:, pt * W:(pt + 1) * W], xk[:, pt * W:(pt + 1) * W],
                rr[:, pt * W:(pt + 1) * W], ALU.mult),
                waits=vdone if pt == 0 else ())
            p.on("vector", lambda e, pt=pt: e.tensor_tensor(
                xk[:, pt * W:(pt + 1) * W], xk[:, pt * W:(pt + 1) * W],
                xch[:, pt * W:(pt + 1) * W], ALU.add),
                inc="s_dve" if pt == PT - 1 else None)
        p.on("gpsimd", lambda e, t0=t0: e.dma_start(out=fm_ap(dr[xdst_name], t0, W),
                                                    in_=xk[:, 0:PT * W]),
             waits=p.now("s_dve"), inc="d_gpsimd", inc_n=16)
        p.bufs["rw_prev"] = p.now("d_gpsimd")
    st.close()
    return p.now("d_gpsimd")


DI = 2048
DTI = DI // 128
NST = 16
NG = 2
NPG = NST // NG


def phase_mamba(p, banks, dr, xsrc_name="xres", xdst_name="xout"):
    W = 512
    NCH = L // W
    Wp = W + 3
    st = ExitStack()
    xch = p.sb("mm_xch", [128, PT * W], stack=st)
    hn = p.sb("mm_hn", [128, PT * W], stack=st)
    rows = p.sb("mm_rows", [1, 3 * W], stack=st)

    xc = p.sb("mm_xc", [128, DTI * W], stack=st)
    xinb = p.sb("mm_xin", [128, DTI * Wp], stack=st)
    yb = p.sb("mm_y", [128, DTI * W], stack=st)
    zu = xinb
    ctxs = p.sb("mm_ctxs", [128, DTI * 4], stack=st)
    bh = p.sb("mm_bh", [128, NPG * W], stack=st)
    dab = p.sb("mm_dab", [128, 2 * W], stack=st)
    dtb_ = p.sb("mm_dt", [128, W], stack=st)
    wtb = p.sb("mm_w", [128, W], stack=st)
    dbc = p.sb("mm_dbc", [128, W], stack=st)
    carry = p.sb("mm_carry", [128, DTI * NST], stack=st)
    wring = p.sb("mm_wring", [128, 2 * 1024], stack=st)
    class _Slice:
        def __init__(self, t, off):
            self.t, self.off = t, off
        def __getitem__(self, key):
            ps, fs = key
            return self.t[ps, fs.start + self.off: fs.stop + self.off]
    p.bufs["mean_b"] = _Slice(dab, 0)
    p.bufs["rstd_b"] = _Slice(dab, W)
    xpw = p.sb("mm_xpw", [128, DTI * 96], stack=st)
    dtpw = p.sb("mm_dtpw", [64, DI], stack=st)
    convw = p.sb("mm_convw", [128, DTI * 4], stack=st)
    convb = p.sb("mm_convb", [128, DTI], stack=st)
    dtbc = p.sb("mm_dtbc", [128, DTI], stack=st)
    Acol = p.sb("mm_A", [128, DTI * NST], stack=st)
    Dcol = p.sb("mm_D", [128, DTI], stack=st)
    nwc = p.sb("mm_nw", [128, PT], stack=st)
    nbc = p.sb("mm_nb", [128, PT], stack=st)
    rtmp = p.sb("mm_rtmp", [128, W], stack=st)
    hres = dr["hres"]
    bcrow = dr["bcrow"]

    def wguard(s):
        return p.bufs.get("mm_ws_%d" % s, [])

    def wset(s, v):
        p.bufs["mm_ws_%d" % s] = v

    for ch in range(NCH):
        t0 = ch * W
        xdone = ln_chunk(p, banks, dr[xsrc_name], t0, W, xch, hn, rows, sq=xc)
        p.on("gpsimd", lambda e, t0=t0: e.dma_start(
            out=fm_ap(hres, t0, W), in_=hn[:, 0:PT * W]),
            waits=xdone, inc="d_gpsimd", inc_n=16)
    p.bufs["ln_prev_dve"] = p.now("d_gpsimd")

    for li in range(4):
        p.dma("sync", xpw[:, 0:DTI * 96],
              bass.AP(dr["xpwT_%d" % li], 0, [[96, 128], [128 * 96, DTI], [1, 96]]))
        p.dma("sync", dtpw[:, 0:DI], dr["dtpwT_%d" % li].ap())
        p.dma("sync", convw[:, :], dr["convw_%d" % li].ap())
        p.dma("sync", convb[:, :], dr["convb_%d" % li].ap())
        p.dma("sync", dtbc[:, :], dr["dtb_%d" % li].ap())
        p.dma("sync", Acol[:, :], dr["Alogc_%d" % li].ap())
        p.dma("sync", Dcol[:, :], dr["Dv_%d" % li].ap())
        p.dma("sync", nwc[:, :], dr["nw_%d" % li].ap())
        p.dma("sync", nbc[:, :], dr["nb_%d" % li].ap())
        wsld = p.now("d_sync")
        p.on("scalar", lambda e: e.activation(out=Acol[:, :], in_=Acol[:, :],
                                              func=AF.Exp), waits=wsld, inc="s_act")
        p.on("vector", lambda e: e.tensor_scalar(Acol[:, :], Acol[:, :], -1.0, None,
                                                 ALU.mult), waits=p.now("s_act"),
             inc="s_dve")
        p.on("vector", lambda e: e.memset(carry[:, :], 0.0), inc="s_dve")
        aw = p.now("s_dve")

        for ch in range(NCH):
            t0 = ch * W
            ctx_ap = bass.AP(xinb.ap().tensor, 0, [[DTI * Wp, 128], [Wp, DTI], [1, 3]])
            if ch == 0:
                p.on("vector", lambda e, ctx_ap=ctx_ap: e.memset(ctx_ap, 0.0),
                     waits=list(aw) + p.bufs.get("mm_prev_cv", []), inc="s_dve")
            else:
                p.on("vector", lambda e, ctx_ap=ctx_ap: e.tensor_copy(
                    ctx_ap, bass.AP(ctxs.ap().tensor, 0, [[DTI * 4, 128], [4, DTI], [1, 3]])),
                    waits=p.bufs.get("mm_prev_cv", []) + p.bufs.get("mm_prev_zevac", []),
                    inc="s_dve")
            ctx_done = p.now("s_dve")
            hdone = ln_chunk(p, banks, hres, t0, W, xch, hn, rows, sq=xc,
                             affine=(nwc, nbc))
            for et in range(DTI):
                slot = et % 2
                p.on("sync", lambda e, et=et, li=li, slot=slot: e.dma_start(
                    out=wring[:, slot * 1024:(slot + 1) * 1024],
                    in_=bass.AP(dr["inwT_%d" % li], et * 128,
                                [[4096, 128], [128 * 4096, PT], [1, 128]])),
                    waits=wguard(slot), inc="d_sync", inc_n=16)
                wld = p.now("d_sync")
                bk = banks[et % 2]
                for kt in range(PT):
                    p.on("tensor", lambda e, et=et, kt=kt, bk=bk, slot=slot: e.matmul(
                        bk.t[:, 0:W],
                        wring[:, slot * 1024 + kt * 128:slot * 1024 + (kt + 1) * 128],
                        hn[:, kt * W:(kt + 1) * W], start=(kt == 0), stop=(kt == PT - 1)),
                        waits=(list(hdone) + list(wld) + bk.free) if kt == 0 else (),
                        inc="s_pe" if kt == PT - 1 else None)
                wset(slot, p.now("s_pe"))
                p.on("scalar", lambda e, et=et, bk=bk: e.activation(
                    out=xinb[:, et * Wp + 3:(et + 1) * Wp], in_=bk.t[:, 0:W], func=AF.Copy),
                    waits=list(p.now("s_pe")) + (list(ctx_done) if et == 0 else []),
                    inc="s_act")
                bk.use(p.now("s_act"))
            xin_done = p.now("s_act")
            for dt in range(DTI):
                p.on("scalar", lambda e, dt=dt: e.activation(
                    out=xc[:, dt * W:(dt + 1) * W], in_=xinb[:, dt * Wp:dt * Wp + W],
                    func=AF.Identity, scale=convw[:, dt * 4:dt * 4 + 1],
                    bias=convb[:, dt:dt + 1]),
                    waits=(xin_done + p.bufs.get("mm_xc_guard", [])) if dt == 0 else (),
                    inc="s_act")
            cv = p.now("s_act")
            p.bufs["mm_prev_cv"] = cv
            for j in range(1, 4):
                for dt in range(DTI):
                    p.on("vector", lambda e, dt=dt, j=j: e.scalar_tensor_tensor(
                        xc[:, dt * W:(dt + 1) * W], xinb[:, dt * Wp + j:dt * Wp + j + W],
                        convw[:, dt * 4 + j:dt * 4 + j + 1], xc[:, dt * W:(dt + 1) * W],
                        ALU.mult, ALU.add),
                        waits=cv if (j == 1 and dt == 0) else (),
                        inc="s_dve" if (j == 3 and dt == DTI - 1) else None)
            p.on("vector", lambda e: e.tensor_copy(
                bass.AP(ctxs.ap().tensor, 0, [[DTI * 4, 128], [4, DTI], [1, 3]]),
                bass.AP(xinb.ap().tensor, W, [[DTI * Wp, 128], [Wp, DTI], [1, 3]])),
                inc="s_dve")
            ctxsave_done = p.now("s_dve")
            p.on("scalar", lambda e: e.activation(out=xc[:, 0:DTI * W], in_=xc[:, 0:DTI * W],
                                                  func=AF.Silu), waits=p.now("s_dve"),
                 inc="s_act")
            xc_done = p.now("s_act")
            b3 = banks[3]
            for kt in range(DTI):
                p.on("tensor", lambda e, kt=kt, b3=b3: e.matmul(
                    b3.t[0:96, 0:W], xpw[:, kt * 96:(kt + 1) * 96], xc[:, kt * W:(kt + 1) * W],
                    start=(kt == 0), stop=(kt == DTI - 1)),
                    waits=(list(xc_done) + list(wsld) + b3.free +
                           p.bufs.get("mm_dbc_guard", [])) if kt == 0 else (),
                    inc="s_pe" if kt == DTI - 1 else None)
            p.on("scalar", lambda e: e.activation(out=dbc[0:96, 0:W], in_=b3.t[0:96, 0:W],
                                                  func=AF.Copy), waits=p.now("s_pe"),
                 inc="s_act")
            b3.use(p.now("s_act"))
            dbc_done = p.now("s_act")
            if "dXC" in dr and li == 0 and ch == 0:
                p.on("gpsimd", lambda e: e.dma_start(out=dr["dXC"].ap(), in_=xc[:, 0:DTI * W]),
                     waits=dbc_done, inc="d_gpsimd", inc_n=16)
                p.on("gpsimd", lambda e: e.dma_start(out=dr["dDBC"].ap(), in_=dbc[0:96, 0:W]),
                     inc="d_gpsimd", inc_n=16)
            for ng in range(NG):
                p.on("gpsimd", lambda e, ng=ng: e.dma_start(
                    out=bass.AP(bcrow, 0, [[W, NPG], [1, W]]),
                    in_=dbc[64 + ng * NPG:64 + (ng + 1) * NPG, 0:W]),
                    waits=list(dbc_done) + p.bufs.get("mm_bcrow_guard", []),
                    inc="d_gpsimd", inc_n=16)
                p.on("gpsimd", lambda e, ng=ng: e.dma_start(
                    out=bass.AP(bcrow, NPG * W, [[W, NPG], [1, W]]),
                    in_=dbc[80 + ng * NPG:80 + (ng + 1) * NPG, 0:W]),
                    inc="d_gpsimd", inc_n=16)
                row_done = p.now("d_gpsimd")
                bc_guard = p.bufs.get("mm_bc_guard", [])
                p.on("gpsimd", lambda e: e.dma_start(
                    out=zu[:, 0:NPG * W],
                    in_=bass.AP(bcrow, 0, [[0, 128], [1, NPG * W]])),
                    waits=list(row_done) + list(bc_guard) + list(xc_done) + list(ctxsave_done),
                    inc="d_gpsimd", inc_n=16)
                p.on("gpsimd", lambda e: e.dma_start(
                    out=zu[:, NPG * W:2 * NPG * W],
                    in_=bass.AP(bcrow, NPG * W, [[0, 128], [1, NPG * W]])),
                    inc="d_gpsimd", inc_n=16)
                bc_done = p.now("d_gpsimd")
                p.bufs["mm_bcrow_guard"] = bc_done
                for dt in range(DTI):
                    b3 = banks[3]
                    p.on("tensor", lambda e, dt=dt, b3=b3: e.matmul(
                        b3.t[:, 0:W], dtpw[:, dt * 128:(dt + 1) * 128], dbc[0:64, 0:W],
                        start=True, stop=True),
                        waits=list(dbc_done) + b3.free + p.bufs.get("mm_dt_guard", []),
                        inc="s_pe")
                    p.on("scalar", lambda e, dt=dt, b3=b3: e.activation(
                        out=dtb_[:, 0:W], in_=b3.t[:, 0:W], func=AF.Identity,
                        bias=dtbc[:, dt:dt + 1]), waits=p.now("s_pe"), inc="s_act")
                    b3.use(p.now("s_act"))
                    p.on("scalar", lambda e: e.activation(out=dtb_[:, 0:W], in_=dtb_[:, 0:W],
                                                          func=AF.Exp), inc="s_act")
                    p.on("scalar", lambda e: e.activation(out=dtb_[:, 0:W], in_=dtb_[:, 0:W],
                                                          func=AF.Ln, bias=1.0), inc="s_act")
                    dt_done = p.now("s_act")
                    if "dDT0" in dr and li == 0 and ch == 0 and ng == 0 and dt in (0, 1):
                        p.on("gpsimd", lambda e, dt=dt: e.dma_start(
                            out=dr["dDT%d" % dt].ap(), in_=dtb_[:, 0:W]),
                            waits=dt_done, inc="d_gpsimd", inc_n=16)
                    if ng == 0:
                        p.on("vector", lambda e, dt=dt: e.tensor_scalar(
                            yb[:, dt * W:(dt + 1) * W], xc[:, dt * W:(dt + 1) * W],
                            Dcol[:, dt:dt + 1], None, ALU.mult),
                            waits=(list(xc_done) + list(aw) +
                                   p.bufs.get("mm_y_guard", [])) if dt == 0 else ())
                    p.on("vector", lambda e, dt=dt: e.tensor_tensor(
                        wtb[:, 0:W], dtb_[:, 0:W], xc[:, dt * W:(dt + 1) * W], ALU.mult),
                        waits=list(dt_done), inc="s_dve")
                    p.on("vector", lambda e: e.tensor_tensor(
                        bh[:, 0:NPG * W],
                        bass.AP(wtb.ap().tensor, 0, [[W, 128], [0, NPG], [1, W]]),
                        zu[:, 0:NPG * W], ALU.mult),
                        waits=list(bc_done), inc="s_dve")
                    b_done = p.now("s_dve")
                    for nl in range(NPG):
                        n = ng * NPG + nl
                        pp = nl % 2
                        p.on("scalar", lambda e, n=n, pp=pp, dt=dt: e.activation(
                            out=dab[:, pp * W:(pp + 1) * W], in_=dtb_[:, 0:W], func=AF.Exp,
                            scale=Acol[:, dt * NST + n:dt * NST + n + 1]),
                            waits=(list(b_done) + p.bufs.get("mm_da_guard_%d" % pp, [])),
                            inc="s_act")
                        p.on("vector", lambda e, nl=nl, pp=pp, n=n, dt=dt: e.tensor_tensor_scan(
                            bh[:, nl * W:(nl + 1) * W], dab[:, pp * W:(pp + 1) * W],
                            bh[:, nl * W:(nl + 1) * W],
                            carry[:, dt * NST + n:dt * NST + n + 1], ALU.mult, ALU.add),
                            waits=p.now("s_act"), inc="s_dve")
                        p.bufs["mm_da_guard_%d" % pp] = p.now("s_dve")
                    dump_gate = []
                    if "dBH0" in dr and li == 0 and ch == 0 and ng == 0 and dt in (0, 1):
                        p.on("gpsimd", lambda e, dt=dt: e.dma_start(
                            out=dr["dBH%d" % dt].ap(), in_=bh[:, 0:NPG * W]),
                            waits=p.now("s_dve"), inc="d_gpsimd", inc_n=16)
                        dump_gate = p.now("d_gpsimd")
                    p.on("vector", lambda e, dt=dt, ng=ng: e.tensor_copy(
                        carry[:, dt * NST + ng * NPG:dt * NST + (ng + 1) * NPG],
                        bass.AP(bh.ap().tensor, W - 1, [[NPG * W, 128], [W, NPG], [1, 1]])),
                        inc="s_dve")
                    p.on("vector", lambda e: e.tensor_tensor(
                        bh[:, 0:NPG * W], bh[:, 0:NPG * W], zu[:, NPG * W:2 * NPG * W],
                        ALU.mult), waits=dump_gate, inc="s_dve")
                    p.on("vector", lambda e: e.tensor_reduce(
                        rtmp[:, 0:W],
                        bass.AP(bh.ap().tensor, 0, [[NPG * W, 128], [1, W], [W, NPG]]),
                        AX.X, ALU.add), inc="s_dve")
                    dump_gate2 = []
                    if "dRT0" in dr and li == 0 and ch == 0 and dt == 0:
                        p.on("gpsimd", lambda e, ng=ng: e.dma_start(
                            out=dr["dRT%d" % ng].ap(), in_=rtmp[:, 0:W]),
                            waits=p.now("s_dve"), inc="d_gpsimd", inc_n=16)
                        dump_gate2 = p.now("d_gpsimd")
                    p.on("vector", lambda e, dt=dt: e.tensor_tensor(
                        yb[:, dt * W:(dt + 1) * W], yb[:, dt * W:(dt + 1) * W],
                        rtmp[:, 0:W], ALU.add),
                        waits=dump_gate2, inc="s_dve")
                    p.bufs["mm_dt_guard"] = p.now("s_dve")
                p.bufs["mm_bc_guard"] = p.now("s_dve")
            scan_done = p.now("s_dve")
            if "dY" in dr and li == 0 and ch == 0:
                p.on("gpsimd", lambda e: e.dma_start(out=dr["dY"].ap(), in_=yb[:, 0:DTI * W]),
                     waits=scan_done, inc="d_gpsimd", inc_n=16)
                scan_done = scan_done + p.now("d_gpsimd")
            for et in range(DTI):
                slot = et % 2
                p.on("sync", lambda e, et=et, li=li, slot=slot: e.dma_start(
                    out=wring[:, slot * 1024:(slot + 1) * 1024],
                    in_=bass.AP(dr["inwT_%d" % li], 2048 + et * 128,
                                [[4096, 128], [128 * 4096, PT], [1, 128]])),
                    waits=wguard(slot), inc="d_sync", inc_n=16)
                wld = p.now("d_sync")
                bk = banks[et % 2]
                for kt in range(PT):
                    p.on("tensor", lambda e, et=et, kt=kt, bk=bk, slot=slot: e.matmul(
                        bk.t[:, 0:W],
                        wring[:, slot * 1024 + kt * 128:slot * 1024 + (kt + 1) * 128],
                        hn[:, kt * W:(kt + 1) * W], start=(kt == 0), stop=(kt == PT - 1)),
                        waits=(list(wld) + bk.free + list(scan_done)) if kt == 0 else (),
                        inc="s_pe" if kt == PT - 1 else None)
                wset(slot, p.now("s_pe"))
                p.on("scalar", lambda e, et=et, bk=bk: e.activation(
                    out=zu[:, et * W:(et + 1) * W], in_=bk.t[:, 0:W], func=AF.Silu),
                    waits=p.now("s_pe"), inc="s_act")
                bk.use(p.now("s_act"))
                p.on("vector", lambda e, et=et: e.tensor_tensor(
                    yb[:, et * W:(et + 1) * W], yb[:, et * W:(et + 1) * W],
                    zu[:, et * W:(et + 1) * W], ALU.mult), waits=p.now("s_act"),
                    inc="s_dve")
            yz_done = p.now("s_dve")
            if "dYZ" in dr and li == 0 and ch == 0:
                p.on("gpsimd", lambda e: e.dma_start(out=dr["dYZ"].ap(), in_=yb[:, 0:DTI * W]),
                     waits=yz_done, inc="d_gpsimd", inc_n=16)
                yz_done = yz_done + p.now("d_gpsimd")
            p.bufs["mm_prev_zevac"] = p.now("s_act")
            p.on("sync", lambda e, t0=t0: e.dma_start(out=xch[:, 0:PT * W],
                                                      in_=fm_ap(hres, t0, W)),
                 waits=yz_done, inc="d_sync", inc_n=16)
            hl = p.now("d_sync")
            for et in range(PT):
                slotp = 0
                p.on("sync", lambda e, et=et, li=li, slotp=slotp: e.dma_start(
                    out=wring[:, slotp * 1024:slotp * 1024 + 2048],
                    in_=bass.AP(dr["outwT_%d" % li], et * 128,
                                [[1024, 128], [128 * 1024, DTI], [1, 128]])),
                    waits=wguard(slotp) + wguard(slotp + 1), inc="d_sync", inc_n=16)
                wld = p.now("d_sync")
                bk = banks[et % 2]
                for kt in range(DTI):
                    p.on("tensor", lambda e, et=et, kt=kt, bk=bk, slotp=slotp: e.matmul(
                        bk.t[:, 0:W],
                        wring[:, slotp * 1024 + kt * 128:slotp * 1024 + (kt + 1) * 128],
                        yb[:, kt * W:(kt + 1) * W], start=(kt == 0), stop=(kt == DTI - 1)),
                        waits=(list(yz_done) + list(wld) + bk.free) if kt == 0 else (),
                        inc="s_pe" if kt == DTI - 1 else None)
                wset(slotp, p.now("s_pe")); wset(slotp + 1, p.now("s_pe"))
                p.on("scalar", lambda e, et=et, bk=bk: e.activation(
                    out=xc[:, et * W:(et + 1) * W], in_=bk.t[:, 0:W], func=AF.Copy),
                    waits=p.now("s_pe"), inc="s_act")
                bk.use(p.now("s_act"))
                p.on("vector", lambda e, et=et: e.tensor_tensor(
                    xch[:, et * W:(et + 1) * W], xch[:, et * W:(et + 1) * W],
                    xc[:, et * W:(et + 1) * W], ALU.add),
                    waits=list(p.now("s_act")) + list(hl), inc="s_dve")
            p.on("gpsimd", lambda e, t0=t0: e.dma_start(out=fm_ap(hres, t0, W),
                                                        in_=xch[:, 0:PT * W]),
                 waits=p.now("s_dve"), inc="d_gpsimd", inc_n=16)
            if "dH0" in dr and ch == 0:
                p.on("gpsimd", lambda e, li=li: e.dma_start(
                    out=dr["dH%d" % li].ap(), in_=xch[:, 0:PT * W]),
                    waits=p.now("d_gpsimd"), inc="d_gpsimd", inc_n=16)
            p.bufs["ln_prev_dve"] = p.now("d_gpsimd")
            p.bufs["ln_prev_pe"] = p.now("s_pe")
            p.bufs["mm_xc_guard"] = p.now("s_dve")
            p.bufs["mm_y_guard"] = p.now("s_pe")
            p.bufs["mm_dbc_guard"] = p.now("s_pe") + p.now("d_gpsimd")

    fw = p.sb("mm_fw", [128, PT], stack=st)
    fb = p.sb("mm_fb", [128, PT], stack=st)
    p.dma("sync", fw[:, :], dr["normfw"].ap())
    p.dma("sync", fb[:, :], dr["normfb"].ap())
    for ch in range(NCH):
        t0 = ch * W
        hdone = ln_chunk(p, banks, hres, t0, W, xch, hn, rows, sq=xc,
                         affine=(fw, fb))
        p.on("sync", lambda e, t0=t0: e.dma_start(out=yb[:, 0:PT * W],
                                                  in_=fm_ap(dr[xsrc_name], t0, W)),
             waits=p.bufs.get("mm_fin_guard", []), inc="d_sync", inc_n=16)
        xl = p.now("d_sync")
        for pt in range(PT):
            p.on("vector", lambda e, pt=pt: e.tensor_tensor(
                yb[:, pt * W:(pt + 1) * W], yb[:, pt * W:(pt + 1) * W],
                hn[:, pt * W:(pt + 1) * W], ALU.add),
                waits=(list(hdone) + list(xl)) if pt == 0 else (),
                inc="s_dve" if pt == PT - 1 else None)
        p.on("gpsimd", lambda e, t0=t0: e.dma_start(out=fm_ap(dr[xdst_name], t0, W),
                                                    in_=yb[:, 0:PT * W]),
             waits=p.now("s_dve"), inc="d_gpsimd", inc_n=16)
        p.bufs["mm_fin_guard"] = p.now("d_gpsimd")
    st.close()
    return p.now("d_gpsimd")


CLOC = 256
NEXP = 32
CAP = 1280
NSLOT = NEXP * CLOC


def phase_moe(p, banks, dr, xsrc_name="xres", xdst_name="xout"):
    W = 512
    NCH = L // W
    st = ExitStack()
    xch = p.sb("mo_xch", [128, PT * W], stack=st)
    xln = p.sb("mo_xln", [128, PT * W], stack=st)
    sq = p.sb("mo_sq", [128, PT * W], stack=st)
    rows = p.sb("mo_rows", [1, 3 * W], stack=st)
    p.bufs["mean_b"] = p.sb("mo_meanb", [128, W], stack=st)
    p.bufs["rstd_b"] = p.sb("mo_rstdb", [128, W], stack=st)
    tmb = p.sb("mo_tmb", [128, D], stack=st)
    wrt = p.sb("mo_wrt", [128, PT * NEXP], stack=st)
    lg = p.sb("mo_lg", [128, NEXP], stack=st)
    sc32 = p.sb("mo_sc32", [128, NEXP], stack=st)
    sc32b = p.sb("mo_sc32b", [128, NEXP], stack=st)
    colA = p.sb("mo_colA", [128, 8], stack=st)
    t1s = p.sb("mo_t1s", [128, 32], stack=st)
    lps = p.sb("mo_lps", [128, 32], stack=st)
    gates = p.sb("mo_gates", [128, 32], stack=st)
    geff = p.sb("mo_geff", [128, 32], stack=st)
    slots_i = p.sb("mo_slots", [128, 32], I32, stack=st)
    toki = p.sb("mo_toki", [128, 32], I32, stack=st)
    offr = p.sb("mo_offr", [1, NEXP], stack=st)
    trow = p.sb("mo_trow", [1, NEXP], stack=st)
    offb = p.sb("mo_offb", [128, NEXP], stack=st)
    allc = p.sb("mo_allc", [8, NEXP], stack=st)
    cmask = p.sb("mo_cmask", [8, 1], stack=st)
    iota_m = p.sb("mo_iotam", [128, NEXP], stack=st)
    iota_p = p.sb("mo_iotap", [128, NEXP], stack=st)
    iota_c = p.sb("mo_iotac", [128, 1], stack=st)
    zrow = p.sb("mo_zrow", [128, D], stack=st)
    idxt = p.sb("mo_idxt", [128, 2], I32, stack=st)
    oh8a = p.sb("mo_oh8", [1, 8], stack=st)
    wg = p.sb("mo_wg", [128, PT * 1024], stack=st)
    wu = p.sb("mo_wu", [128, PT * 1024], stack=st)
    wd = p.sb("mo_wd", [128, PT * 1024], stack=st)
    bufF = p.sb("mo_bufF", [128, PT * CLOC], stack=st)
    sg = p.sb("mo_sg", [128, PT * CLOC], stack=st)
    su = p.sb("mo_su", [128, PT * CLOC], stack=st)
    ln1_f, ln1_tm, out_tm = dr["ln1_f"], dr["ln1_tm"], dr["out_tm"]
    s2t, cntd, cntd_out = dr["slot2tok"], dr["cnt_in"], dr["cnt_out"]

    p.dma("sync", wrt[:, 0:PT * NEXP],
          bass.AP(dr["w_router"], 0, [[NEXP, 128], [128 * NEXP, PT], [1, NEXP]]))
    p.dma("sync", cmask[:, :], dr["coremask"].ap())
    p.dma("sync", iota_m[:, :], dr["iota_m"].ap())
    p.dma("sync", iota_p[:, :], dr["iota_p"].ap())
    p.dma("sync", iota_c[:, :], dr["iota_c"].ap())
    wld = p.now("d_sync")
    p.on("vector", lambda e: e.memset(zrow[:, :], 0.0), inc="s_dve")
    zr = p.now("s_dve")
    for blk in range((L + 128) // 128):
        r0 = blk * 128
        nr = min(128, L + 1 - r0)
        p.on("gpsimd", lambda e, r0=r0, nr=nr: e.dma_start(
            out=bass.AP(out_tm, r0 * D, [[D, nr], [1, D]]), in_=zrow[0:nr, :]),
            waits=zr if blk == 0 else (), inc="d_gpsimd", inc_n=16)
    p.on("gpsimd", lambda e: e.dma_start(
        out=bass.AP(ln1_tm, L * D, [[D, 1], [1, D]]), in_=zrow[0:1, :]),
        waits=zr, inc="d_gpsimd", inc_n=16)
    p.on("vector", lambda e: e.memset(zrow[:, 0:66].bitcast(I32), L), waits=p.now("d_gpsimd"),
         inc="s_dve")
    p.on("gpsimd", lambda e: e.dma_start(
        out=bass.AP(s2t, 0, [[66, 128], [1, 66]]), in_=zrow[:, 0:66].bitcast(I32)),
        waits=p.now("s_dve"), inc="d_gpsimd", inc_n=16)
    init_done = p.now("d_gpsimd")

    b4 = banks[4]
    for ch in range(NCH):
        t0 = ch * W
        xdone = ln_chunk(p, banks, dr[xsrc_name], t0, W, xch, xln, rows, sq=sq)
        p.on("gpsimd", lambda e, t0=t0: e.dma_start(out=fm_ap(ln1_f, t0, W),
                                                    in_=xln[:, 0:PT * W]),
             waits=xdone, inc="d_gpsimd", inc_n=16)
        for tl in range(W // 128):
            for pt in range(PT):
                p.on("tensor", lambda e, tl=tl, pt=pt: e.transpose(
                    b4.t[:, 0:128], xln[:, pt * W + tl * 128:pt * W + (tl + 1) * 128],
                    p.consts["ident"][:, :]),
                    waits=(list(xdone) + b4.free +
                           (p.bufs.get("mo_tmb_guard", []) if pt == 0 else [])),
                    inc="s_pe")
                p.on("scalar", lambda e, pt=pt: e.activation(
                    out=tmb[:, pt * 128:(pt + 1) * 128], in_=b4.t[:, 0:128], func=AF.Copy),
                    waits=p.now("s_pe"), inc="s_act")
                b4.use(p.now("s_act"))
            gt = ch * 4 + tl
            p.on("gpsimd", lambda e, gt=gt: e.dma_start(
                out=bass.AP(ln1_tm, gt * 128 * D, [[D, 128], [1, D]]), in_=tmb[:, :]),
                waits=p.now("s_act"), inc="d_gpsimd", inc_n=16)
            p.bufs["mo_tmb_guard"] = p.now("d_gpsimd")
    ln_done = p.now("d_gpsimd")

    p.paranoid = True
    b5, b6 = banks[5], banks[6]
    p.on("vector", lambda e: e.memset(offr[0:1, :], 0.0), inc="s_dve")
    p.bufs["mo_offr_ready"] = p.now("s_dve")
    for tl in range(32):
        p.on("sync", lambda e, tl=tl: e.dma_start(
            out=xch[:, 0:PT * 128], in_=fm_ap(ln1_f, tl * 128, 128)),
            waits=p.bufs.get("mo_rt_guard", []) + (list(ln_done) if tl == 0 else []),
            inc="d_sync", inc_n=16)
        ld = p.now("d_sync")
        for pt in range(PT):
            p.on("tensor", lambda e, pt=pt: e.matmul(
                b5.t[:, 0:NEXP], xch[:, pt * 128:(pt + 1) * 128],
                wrt[:, pt * NEXP:(pt + 1) * NEXP], start=(pt == 0), stop=(pt == PT - 1)),
                waits=(list(ld) + list(wld) + b5.free) if pt == 0 else (),
                inc="s_pe" if pt == PT - 1 else None)
        p.on("scalar", lambda e: e.activation(out=lg[:, :], in_=b5.t[:, 0:NEXP],
                                              func=AF.Copy),
             waits=list(p.now("s_pe")) + list(p.now("s_dve")), inc="s_act")
        b5.use(p.now("s_act"))
        lgd = p.now("s_act")
        p.on("vector", lambda e: e.tensor_reduce(colA[:, 0:1], lg[:, :], AX.X, ALU.max),
             waits=lgd, inc="s_dve")
        p.on("vector", lambda e: e.tensor_scalar(sc32[:, :], lg[:, :], colA[:, 0:1], None,
                                                 ALU.subtract), inc="s_dve")
        p.on("vector", lambda e: e.tensor_scalar(sc32b[:, :], sc32[:, :], 0.0, None,
                                                 ALU.is_ge), inc="s_dve")
        p.on("vector", lambda e: e.tensor_tensor(sc32b[:, :], sc32b[:, :], iota_m[:, :],
                                                 ALU.mult), inc="s_dve")
        gate_d = []
        if "dSEL" in dr and tl == 0:
            p.on("gpsimd", lambda e: e.dma_start(out=dr["dSEL"].ap(), in_=sc32b[:, :]),
                 waits=p.now("s_dve"), inc="d_gpsimd", inc_n=16)
            p.on("gpsimd", lambda e: e.dma_start(out=dr["dIOM"].ap(), in_=iota_m[:, :]),
                 inc="d_gpsimd", inc_n=16)
            p.on("gpsimd", lambda e: e.dma_start(out=dr["dLG"].ap(), in_=lg[:, :]),
                 inc="d_gpsimd", inc_n=16)
            gate_d = p.now("d_gpsimd")
        p.on("vector", lambda e: e.tensor_reduce(colA[:, 1:2], sc32b[:, :], AX.X, ALU.min),
             waits=gate_d, inc="s_dve")
        p.on("vector", lambda e, tl=tl: e.tensor_scalar(t1s[:, tl:tl + 1], colA[:, 1:2],
                                                        10000.0, None, ALU.add),
             inc="s_dve")
        p.on("scalar", lambda e: e.activation(out=sc32[:, :], in_=sc32[:, :], func=AF.Exp),
             waits=p.now("s_dve"), inc="s_act")
        p.on("vector", lambda e: e.tensor_reduce(colA[:, 2:3], sc32[:, :], AX.X, ALU.add),
             waits=p.now("s_act"), inc="s_dve")
        p.on("scalar", lambda e: e.activation(out=colA[:, 3:4], in_=colA[:, 2:3],
                                              func=AF.Ln), waits=p.now("s_dve"),
             inc="s_act")
        p.on("scalar", lambda e, tl=tl: e.activation(out=gates[:, tl:tl + 1],
                                                     in_=colA[:, 3:4], func=AF.Exp,
                                                     scale=-1.0), inc="s_act")
        p.on("vector", lambda e, tl=tl: e.tensor_scalar(sc32[:, :], iota_p[:, :],
                                                        t1s[:, tl:tl + 1], None,
                                                        ALU.subtract),
             waits=p.now("s_act"), inc="s_dve")
        p.on("vector", lambda e: e.tensor_scalar(sc32[:, :], sc32[:, :], 0.0, None,
                                                 ALU.is_equal), inc="s_dve")
        oh = p.now("s_dve")
        p.on("tensor", lambda e: e.matmul(b6.t[:, 0:NEXP], p.consts["tri"][:, :],
                                          sc32[:, :], start=True, stop=False),
             waits=(list(oh) + b6.free), inc="s_pe")
        p.on("tensor", lambda e: e.matmul(b6.t[:, 0:NEXP], p.consts["ones_row"][0:1, :],
                                          offr[0:1, :], start=False, stop=True),
             waits=p.bufs.get("mo_offr_ready", []), inc="s_pe")
        p.on("scalar", lambda e: e.activation(out=sc32b[:, :], in_=b6.t[:, 0:NEXP],
                                              func=AF.Copy),
             waits=list(p.now("s_pe")) + list(p.now("s_dve")), inc="s_act")
        b6.use(p.now("s_act"))
        b7r = banks[7]
        offs_mm_pe = p.now("s_pe")
        p.on("tensor", lambda e, tl=tl, b7r=b7r: e.matmul(
            b7r.t[0:1, 0:NEXP], p.consts["ones_col"][:, 0:1], sc32[:, :],
            start=True, stop=True),
            waits=b7r.free, inc="s_pe")
        p.on("scalar", lambda e, b7r=b7r: e.activation(out=trow[0:1, :],
                                                       in_=b7r.t[0:1, 0:NEXP],
                                                       func=AF.Copy),
             waits=p.now("s_pe"), inc="s_act")
        b7r.use(p.now("s_act"))
        p.on("vector", lambda e: e.tensor_tensor(offr[0:1, :], offr[0:1, :],
                                                 trow[0:1, :], ALU.add),
             waits=list(p.now("s_act")) + offs_mm_pe, inc="s_dve")
        p.bufs["mo_offr_ready"] = p.now("s_dve")
        p.on("vector", lambda e: e.tensor_tensor(sc32b[:, :], sc32b[:, :], sc32[:, :],
                                                 ALU.mult), waits=p.now("s_act"),
             inc="s_dve")
        p.on("vector", lambda e: e.tensor_reduce(colA[:, 4:5], sc32b[:, :], AX.X, ALU.add),
             inc="s_dve")
        p.on("vector", lambda e, tl=tl: e.tensor_scalar(lps[:, tl:tl + 1], colA[:, 4:5],
                                                        -1.0, None, ALU.add),
             inc="s_dve")
        p.bufs["mo_rt_dve"] = p.now("s_dve")
        p.bufs["mo_rt_guard"] = p.now("s_dve") + p.now("s_pe")
    oh8 = oh8a
    p.dma("sync", oh8[:, :], dr["onehot8"].ap())
    cidld = p.now("d_sync")
    b6m = banks[6]
    p.on("tensor", lambda e: e.matmul(b6m.t[0:8, 0:NEXP], oh8[0:1, :], offr[0:1, :],
                                      start=True, stop=True),
         waits=list(cidld) + list(p.now("s_dve")) + b6m.free, inc="s_pe")
    p.on("scalar", lambda e: e.activation(out=allc[:, :], in_=b6m.t[0:8, 0:NEXP],
                                          func=AF.Copy), waits=p.now("s_pe"), inc="s_act")
    b6m.use(p.now("s_act"))
    p.on("gpsimd", lambda e: e.dma_start(out=cntd.ap(), in_=allc[:, :]),
         waits=p.now("s_act"), inc="d_gpsimd", inc_n=16)
    p.on("gpsimd", lambda e: e.collective_compute(
        "AllReduce", mybir.AluOpType.add, replica_groups=[list(range(NCORE))],
        ins=[cntd.ap()], outs=[cntd_out.ap()]),
        waits=p.now("d_gpsimd"), inc="s_cc")
    p.on("gpsimd", lambda e: e.dma_start(out=allc[:, :], in_=cntd_out.ap()),
         waits=p.now("s_cc"), inc="d_gpsimd", inc_n=16)
    b5 = banks[5]
    p.on("tensor", lambda e: e.matmul(b5.t[0:1, 0:NEXP], cmask[:, 0:1], allc[:, :],
                                      start=True, stop=True),
         waits=(list(p.now("d_gpsimd")) + b5.free), inc="s_pe")
    p.on("scalar", lambda e: e.activation(out=offr[0:1, :], in_=b5.t[0:1, 0:NEXP],
                                          func=AF.Copy), waits=p.now("s_pe"), inc="s_act")
    p.on("tensor", lambda e: e.matmul(b5.t[:, 0:NEXP], p.consts["ones_row"][0:1, :],
                                      offr[0:1, :], start=True, stop=True),
         waits=p.now("s_act"), inc="s_pe")
    p.on("scalar", lambda e: e.activation(out=offb[:, :], in_=b5.t[:, 0:NEXP],
                                          func=AF.Copy), waits=p.now("s_pe"), inc="s_act")
    b5.use(p.now("s_act"))
    offb_done = p.now("s_act")

    for tl in range(32):
        p.on("vector", lambda e, tl=tl: e.tensor_scalar(sc32[:, :], iota_p[:, :],
                                                        t1s[:, tl:tl + 1], None,
                                                        ALU.subtract),
             waits=list(offb_done) if tl == 0 else (), inc="s_dve")
        p.on("vector", lambda e: e.tensor_scalar(sc32[:, :], sc32[:, :], 0.0, None,
                                                 ALU.is_equal), inc="s_dve")
        p.on("vector", lambda e: e.tensor_tensor(sc32[:, :], sc32[:, :], offb[:, :],
                                                 ALU.mult), inc="s_dve")
        p.on("vector", lambda e: e.tensor_reduce(colA[:, 0:1], sc32[:, :], AX.X, ALU.add),
             inc="s_dve")
        p.on("vector", lambda e, tl=tl: e.tensor_tensor(colA[:, 1:2], lps[:, tl:tl + 1],
                                                        colA[:, 0:1], ALU.add),
             inc="s_dve")
        p.on("vector", lambda e: e.tensor_scalar(colA[:, 2:3], colA[:, 1:2], float(CAP),
                                                 None, ALU.is_lt), inc="s_dve")
        p.on("vector", lambda e, tl=tl: e.tensor_scalar(colA[:, 3:4], lps[:, tl:tl + 1],
                                                        float(CLOC), None, ALU.is_lt),
             inc="s_dve")
        p.on("vector", lambda e: e.tensor_tensor(colA[:, 3:4], colA[:, 3:4], colA[:, 2:3],
                                                 ALU.mult), inc="s_dve")
        p.on("vector", lambda e, tl=tl: e.tensor_tensor(geff[:, tl:tl + 1],
                                                        gates[:, tl:tl + 1], colA[:, 2:3],
                                                        ALU.mult), inc="s_dve")
        p.on("vector", lambda e, tl=tl: e.tensor_scalar(colA[:, 5:6], t1s[:, tl:tl + 1],
                                                        float(CLOC), None, ALU.mult),
             inc="s_dve")
        p.on("vector", lambda e, tl=tl: e.tensor_tensor(colA[:, 5:6], colA[:, 5:6],
                                                        lps[:, tl:tl + 1], ALU.add),
             inc="s_dve")
        p.on("vector", lambda e: e.tensor_scalar(colA[:, 5:6], colA[:, 5:6],
                                                 float(NSLOT), None, ALU.subtract),
             inc="s_dve")
        p.on("vector", lambda e: e.tensor_tensor(colA[:, 5:6], colA[:, 5:6], colA[:, 3:4],
                                                 ALU.mult), inc="s_dve")
        p.on("vector", lambda e: e.tensor_scalar(colA[:, 5:6], colA[:, 5:6],
                                                 float(NSLOT), None, ALU.add),
             inc="s_dve")
        p.on("vector", lambda e: e.tensor_scalar(colA[:, 5:6], colA[:, 5:6], 0.0,
                                                 float(NSLOT), ALU.max, ALU.min),
             inc="s_dve")
        p.on("vector", lambda e, tl=tl: e.tensor_copy(slots_i[:, tl:tl + 1],
                                                      colA[:, 5:6]), inc="s_dve")
        p.on("vector", lambda e, tl=tl: e.tensor_scalar(colA[:, 6:7], iota_c[:, 0:1],
                                                        float(tl * 128), None, ALU.add),
             inc="s_dve")
        p.on("vector", lambda e, tl=tl: e.tensor_copy(toki[:, tl:tl + 1], colA[:, 6:7]),
             inc="s_dve")
        p.on("gpsimd", lambda e, tl=tl: e.indirect_dma_start(
            out=bass.AP(s2t, 0, [[1, NSLOT + 256], [1, 1]]),
            out_offset=bass.IndirectOffsetOnAxis(ap=slots_i[:, tl:tl + 1], axis=0),
            in_=toki[:, tl:tl + 1], in_offset=None),
            waits=list(p.now("s_dve")) + list(init_done), inc="d_gpsimd", inc_n=16)
    p.paranoid = False
    route_done = p.now("d_gpsimd")

    for ex in range(NEXP):
        for wsb, nm in ((wg, "w_gate"), (wu, "w_up"), (wd, "w_down")):
            p.on("sync", lambda e, wsb=wsb, nm=nm, ex=ex: e.dma_start(
                out=wsb[:, 0:PT * 1024],
                in_=bass.AP(dr[nm], ex * 1024 * 1024,
                            [[1024, 128], [128 * 1024, PT], [1, 1024]])),
                waits=p.bufs.get("mo_w_guard", []), inc="d_sync", inc_n=16)
        wldx = p.now("d_sync")
        for ct in range(CLOC // 128):
            p.on("sync", lambda e, ex=ex, ct=ct: e.dma_start(
                out=idxt[:, ct:ct + 1],
                in_=bass.AP(s2t, ex * CLOC + ct * 128, [[1, 128], [1, 1]])),
                waits=(list(route_done) + p.bufs.get("mo_idx_guard", [])),
                inc="d_sync", inc_n=16)
            p.on("gpsimd", lambda e, ct=ct: e.indirect_dma_start(
                out=tmb[:, :], out_offset=None,
                in_=bass.AP(ln1_tm, 0, [[D, L + 1], [1, D]]),
                in_offset=bass.IndirectOffsetOnAxis(ap=idxt[:, ct:ct + 1], axis=0)),
                waits=list(p.now("d_sync")) + p.bufs.get("mo_tmb_guard", []),
                inc="d_gpsimd", inc_n=16)
            gdone = p.now("d_gpsimd")
            for pt in range(PT):
                p.on("tensor", lambda e, pt=pt: e.transpose(
                    b4.t[:, 0:128], tmb[:, pt * 128:(pt + 1) * 128],
                    p.consts["ident"][:, :]), waits=(list(gdone) + b4.free), inc="s_pe")
                p.on("scalar", lambda e, pt=pt, ct=ct: e.activation(
                    out=bufF[:, pt * CLOC + ct * 128:pt * CLOC + (ct + 1) * 128],
                    in_=b4.t[:, 0:128], func=AF.Copy), waits=p.now("s_pe"), inc="s_act")
                b4.use(p.now("s_act"))
            p.bufs["mo_tmb_guard"] = p.now("s_pe")
        buf_done = p.now("s_act")
        for dst, wsb, func in ((sg, wg, AF.Silu), (su, wu, AF.Copy)):
            for ft in range(PT):
                bk = banks[ft % 2]
                for kt in range(PT):
                    p.on("tensor", lambda e, ft=ft, kt=kt, bk=bk, wsb=wsb: e.matmul(
                        bk.t[:, 0:CLOC],
                        wsb[:, kt * 1024 + ft * 128:kt * 1024 + (ft + 1) * 128],
                        bufF[:, kt * CLOC:(kt + 1) * CLOC], start=(kt == 0),
                        stop=(kt == PT - 1)),
                        waits=(list(buf_done) + list(wldx) + bk.free) if kt == 0 else (),
                        inc="s_pe" if kt == PT - 1 else None)
                p.on("scalar", lambda e, ft=ft, bk=bk, dst=dst, func=func: e.activation(
                    out=dst[:, ft * CLOC:(ft + 1) * CLOC], in_=bk.t[:, 0:CLOC], func=func),
                    waits=p.now("s_pe"), inc="s_act")
                bk.use(p.now("s_act"))
        gu = p.now("s_act")
        for ft in range(PT):
            p.on("vector", lambda e, ft=ft: e.tensor_tensor(
                sg[:, ft * CLOC:(ft + 1) * CLOC], sg[:, ft * CLOC:(ft + 1) * CLOC],
                su[:, ft * CLOC:(ft + 1) * CLOC], ALU.mult),
                waits=gu if ft == 0 else (), inc="s_dve" if ft == PT - 1 else None)
        act_done = p.now("s_dve")
        for ht in range(PT):
            bk = banks[ht % 2]
            for kt in range(PT):
                p.on("tensor", lambda e, ht=ht, kt=kt, bk=bk: e.matmul(
                    bk.t[:, 0:CLOC],
                    wd[:, kt * 1024 + ht * 128:kt * 1024 + (ht + 1) * 128],
                    sg[:, kt * CLOC:(kt + 1) * CLOC], start=(kt == 0),
                    stop=(kt == PT - 1)),
                    waits=(list(act_done) + bk.free) if kt == 0 else (),
                    inc="s_pe" if kt == PT - 1 else None)
            p.on("scalar", lambda e, ht=ht, bk=bk: e.activation(
                out=su[:, ht * CLOC:(ht + 1) * CLOC], in_=bk.t[:, 0:CLOC], func=AF.Copy),
                waits=p.now("s_pe"), inc="s_act")
            bk.use(p.now("s_act"))
        down_done = p.now("s_act")
        for ct in range(CLOC // 128):
            for pt in range(PT):
                p.on("tensor", lambda e, pt=pt, ct=ct: e.transpose(
                    b4.t[:, 0:128], su[:, pt * CLOC + ct * 128:pt * CLOC + (ct + 1) * 128],
                    p.consts["ident"][:, :]),
                    waits=(list(down_done) + b4.free + p.bufs.get("mo_tmb_guard2", []))
                    if pt == 0 else b4.free, inc="s_pe")
                p.on("scalar", lambda e, pt=pt: e.activation(
                    out=tmb[:, pt * 128:(pt + 1) * 128], in_=b4.t[:, 0:128], func=AF.Copy),
                    waits=p.now("s_pe"), inc="s_act")
                b4.use(p.now("s_act"))
            p.on("gpsimd", lambda e, ct=ct: e.indirect_dma_start(
                out=bass.AP(out_tm, 0, [[D, L + 1], [1, D]]),
                out_offset=bass.IndirectOffsetOnAxis(ap=idxt[:, ct:ct + 1], axis=0),
                in_=tmb[:, :], in_offset=None),
                waits=list(p.now("s_act")) + list(init_done), inc="d_gpsimd", inc_n=16)
            p.bufs["mo_tmb_guard2"] = p.now("d_gpsimd")
            p.bufs["mo_tmb_guard"] = p.now("d_gpsimd")
        p.bufs["mo_idx_guard"] = p.now("d_gpsimd")
        p.bufs["mo_w_guard"] = p.now("s_pe")
    sc_done = p.now("d_gpsimd")
    if "dT1" in dr:
        for nm, t in (("dT1", t1s), ("dLP", lps), ("dGE", geff), ("dAC", allc)):
            p.on("gpsimd", lambda e, nm=nm, t=t: e.dma_start(out=dr[nm].ap(), in_=t[:, :]),
                 waits=sc_done, inc="d_gpsimd", inc_n=16)
        sc_done = p.now("d_gpsimd")

    for ch in range(NCH):
        t0 = ch * W
        p.on("sync", lambda e, t0=t0: e.dma_start(out=xch[:, 0:PT * W],
                                                  in_=fm_ap(dr[xsrc_name], t0, W)),
             waits=p.bufs.get("mo_fin_guard", []), inc="d_sync", inc_n=16)
        p.on("sync", lambda e, t0=t0: e.dma_start(out=xln[:, 0:PT * W],
                                                  in_=fm_ap(ln1_f, t0, W)),
             inc="d_sync", inc_n=16)
        ld = p.now("d_sync")
        for tl in range(W // 128):
            gt = ch * 4 + tl
            p.on("sync", lambda e, gt=gt: e.dma_start(
                out=tmb[:, :], in_=bass.AP(out_tm, gt * 128 * D, [[D, 128], [1, D]])),
                waits=(list(sc_done) + p.bufs.get("mo_tmb_guard", [])),
                inc="d_sync", inc_n=16)
            p.bufs["mo_tmb_guard"] = []
            gl = p.now("d_sync")
            p.on("vector", lambda e, gt=gt: e.tensor_scalar(
                tmb[:, :], tmb[:, :], geff[:, gt:gt + 1], None, ALU.mult),
                waits=gl, inc="s_dve")
            sd = p.now("s_dve")
            for pt in range(PT):
                p.on("tensor", lambda e, pt=pt, tl=tl: e.transpose(
                    b4.t[:, 0:128], tmb[:, pt * 128:(pt + 1) * 128],
                    p.consts["ident"][:, :]), waits=(list(sd) + b4.free), inc="s_pe")
                p.on("scalar", lambda e, pt=pt, tl=tl: e.activation(
                    out=sq[:, pt * W + tl * 128:pt * W + (tl + 1) * 128],
                    in_=b4.t[:, 0:128], func=AF.Copy), waits=p.now("s_pe"), inc="s_act")
                b4.use(p.now("s_act"))
            p.bufs["mo_tmb_guard"] = p.now("s_pe")
        moef = p.now("s_act")
        for pt in range(PT):
            p.on("vector", lambda e, pt=pt: e.tensor_tensor(
                xch[:, pt * W:(pt + 1) * W], xch[:, pt * W:(pt + 1) * W],
                xln[:, pt * W:(pt + 1) * W], ALU.add),
                waits=(list(moef) + list(ld)) if pt == 0 else ())
            p.on("vector", lambda e, pt=pt: e.tensor_tensor(
                xch[:, pt * W:(pt + 1) * W], xch[:, pt * W:(pt + 1) * W],
                sq[:, pt * W:(pt + 1) * W], ALU.add),
                inc="s_dve" if pt == PT - 1 else None)
        p.on("gpsimd", lambda e, t0=t0: e.dma_start(out=fm_ap(dr[xdst_name], t0, W),
                                                    in_=xch[:, 0:PT * W]),
             waits=p.now("s_dve"), inc="d_gpsimd", inc_n=16)
        p.bufs["mo_fin_guard"] = p.now("d_gpsimd")
    st.close()
    return p.now("d_gpsimd")


from concourse.bass_utils import run_bass_kernel_spmd

NL = 4


def build_program():
    nc = bass.Bass("TRN2", target_bir_lowering=False)
    p = Prog(nc)
    dr = {}
    dr["x_in"] = nc.dram_tensor("x_in", [D, L], F32, kind="ExternalInput")
    dr["out"] = nc.dram_tensor("out", [D, L], F32, kind="ExternalOutput")
    for nm in ("xa", "xm", "xq", "hres", "ln1_f", "lnbuf"):
        dr[nm] = nc.dram_tensor(nm, [D, L], F32)
    dr["ln1_tm"] = nc.dram_tensor("ln1_tm", [L + 1, D], F32)
    dr["out_tm"] = nc.dram_tensor("out_tm", [L + 1, D], F32)
    dr["slot2tok"] = nc.dram_tensor("slot2tok", [NSLOT + 256], I32)
    dr["cnt_in"] = nc.dram_tensor("cnt_in", [8, NEXP], F32)
    dr["cnt_out"] = nc.dram_tensor("cnt_out", [8, NEXP], F32)
    dr["bcrow"] = nc.dram_tensor("bcrow", [2, NPG * 512], F32)
    ext = [
        ("wq", [D, 64], F32), ("wk", [D, 64], F32), ("wv", [D, 64], F32),
        ("wo", [64, D], F32), ("bq", [64], F32), ("bk", [64], F32),
        ("bv", [64], F32), ("bo_t", [128, PT], F32), ("maskdiag", [128, 128], F32),
        ("w_router", [D, NEXP], F32), ("w_gate", [NEXP, 1024, 1024], F32),
        ("w_up", [NEXP, 1024, 1024], F32), ("w_down", [NEXP, 1024, 1024], F32),
        ("coremask", [8, 1], F32), ("onehot8", [1, 8], F32),
        ("iota_m", [128, NEXP], F32), ("iota_p", [128, NEXP], F32),
        ("iota_c", [128, 1], F32),
        ("normfw", [128, PT], F32), ("normfb", [128, PT], F32),
        ("wrT", [D, D], F32), ("wkT", [D, D], F32), ("wvT", [D, D], F32),
        ("mur_t", [128, PT], F32), ("muk_t", [128, PT], F32),
        ("ones_col", [128, 1], F32), ("ones_row", [1, 128], F32),
        ("ident", [128, 128], F32), ("tri", [128, 128], F32),
        ("cscal", [128, 8], F32),
    ]
    for i in range(NL):
        ext += [("inwT_%d" % i, [1024, 4096], F32), ("xpwT_%d" % i, [2048, 96], F32),
                ("dtpwT_%d" % i, [64, 2048], F32), ("outwT_%d" % i, [2048, 1024], F32),
                ("convw_%d" % i, [128, DTI * 4], F32), ("convb_%d" % i, [128, DTI], F32),
                ("dtb_%d" % i, [128, DTI], F32), ("Alogc_%d" % i, [128, DTI * NST], F32),
                ("Dv_%d" % i, [128, DTI], F32), ("nw_%d" % i, [128, PT], F32),
                ("nb_%d" % i, [128, PT], F32)]
    for n, sh, dt_ in ext:
        dr[n] = nc.dram_tensor(n, sh, dt_, kind="ExternalInput")
    for cn in ("ones_col", "ones_row", "ident", "tri", "cscal"):
        p.consts[cn] = p.sb("c_" + cn, list(dr[cn].shape))
        p.dma("sync", p.consts[cn][:, :], dr[cn].ap())
    banks = [Bank(p.ps("pb%d" % i)) for i in range(8)]
    phase_attn(p, banks, dr, "x_in", "xa")
    p.barrier()
    phase_moe(p, banks, dr, "xa", "xm")
    p.barrier()
    phase_mamba(p, banks, dr, "xm", "xq")
    p.barrier()
    phase_rwkv(p, banks, dr, "xq", "out")
    p.on("gpsimd", lambda e: None, waits=p.now("d_gpsimd") + p.now("d_sync"))
    p.on("sync", lambda e: None, waits=p.now("d_sync") + p.now("d_gpsimd"))
    p.finalize()
    return nc


def _colt(v):
    return np.ascontiguousarray(np.asarray(v, np.float32).reshape(-1, 128).T)


def kernel(**inputs):
    gi = lambda n: np.asarray(inputs[n], np.float32)
    x = gi("L0_attn_x")
    nc = build_program()
    common = {
        "wq": gi("L0_attn_wq"), "wk": gi("L0_attn_wk"), "wv": gi("L0_attn_wv"),
        "wo": gi("L0_attn_wo"), "bq": gi("L0_attn_bq"), "bk": gi("L0_attn_bk"),
        "bv": gi("L0_attn_bv"), "bo_t": _colt(gi("L0_attn_bo")),
        "maskdiag": np.triu(np.ones((128, 128), np.float32), 1) * np.float32(-1e11),
        "w_router": gi("L0_ffn_w_router"), "w_gate": gi("L0_ffn_w_gate"),
        "w_up": gi("L0_ffn_w_up"), "w_down": gi("L0_ffn_w_down"),
        "iota_m": np.broadcast_to(np.arange(NEXP, dtype=np.float32) - 10000.0,
                                  (128, NEXP)).copy(),
        "iota_p": np.broadcast_to(np.arange(NEXP, dtype=np.float32), (128, NEXP)).copy(),
        "iota_c": np.arange(128, dtype=np.float32).reshape(128, 1).copy(),
        "normfw": _colt(gi("L1_attn_normf_w")), "normfb": _colt(gi("L1_attn_normf_b")),
        "wrT": np.ascontiguousarray(gi("L1_ffn_W_r").T),
        "wkT": np.ascontiguousarray(gi("L1_ffn_W_k").T),
        "wvT": np.ascontiguousarray(gi("L1_ffn_W_v").T),
        "mur_t": _colt(gi("L1_ffn_mu_r")), "muk_t": _colt(gi("L1_ffn_mu_k")),
        "ones_col": np.ones((128, 1), np.float32),
        "ones_row": np.ones((1, 128), np.float32),
        "ident": np.eye(128, dtype=np.float32),
        "tri": np.ascontiguousarray(np.tril(np.ones((128, 128), np.float32)).T),
        "cscal": np.broadcast_to(np.array([1e-5] + [0] * 7, np.float32),
                                 (128, 8)).copy(),
    }
    inw = gi("L1_attn_in_proj_w"); xpw = gi("L1_attn_x_proj_w")
    dtpw = gi("L1_attn_dt_proj_w"); outw = gi("L1_attn_out_proj_w")
    convw = gi("L1_attn_conv_w"); convb = gi("L1_attn_conv_b")
    dtpb = gi("L1_attn_dt_proj_b"); Alog = gi("L1_attn_A_log")
    Dvv = gi("L1_attn_D"); nw = gi("L1_attn_norm_w"); nb = gi("L1_attn_norm_b")
    for i in range(NL):
        common["inwT_%d" % i] = np.ascontiguousarray(inw[i].T)
        common["xpwT_%d" % i] = np.ascontiguousarray(xpw[i].T)
        common["dtpwT_%d" % i] = np.ascontiguousarray(dtpw[i].T)
        common["outwT_%d" % i] = np.ascontiguousarray(outw[i].T)
        common["convw_%d" % i] = np.ascontiguousarray(
            convw[i].reshape(DTI, 128, 4).transpose(1, 0, 2).reshape(128, DTI * 4))
        common["convb_%d" % i] = _colt(convb[i])
        common["dtb_%d" % i] = _colt(dtpb[i])
        common["Alogc_%d" % i] = np.ascontiguousarray(
            Alog[i].reshape(DTI, 128, NST).transpose(1, 0, 2).reshape(128, DTI * NST))
        common["Dv_%d" % i] = _colt(Dvv[i])
        common["nw_%d" % i] = _colt(nw[i])
        common["nb_%d" % i] = _colt(nb[i])
    in_maps = []
    for c in range(NCORE):
        m = dict(common)
        m["x_in"] = np.ascontiguousarray(x[c].T)
        m["coremask"] = (np.arange(8) < c).astype(np.float32).reshape(8, 1)
        m["onehot8"] = (np.arange(8) == c).astype(np.float32).reshape(1, 8)
        in_maps.append(m)
    res = run_bass_kernel_spmd(nc, in_maps, core_ids=list(range(NCORE)))
    out = np.stack([np.ascontiguousarray(res.results[c]["out"].T)
                    for c in range(NCORE)], axis=0)
    return out
